# version 37
# baseline (speedup 1.0000x reference)
import os
import numpy as np
import ml_dtypes
import concourse.bass as bass
import concourse.mybir as mybir
from concourse.bass_utils import run_bass_kernel_spmd

F32 = mybir.dt.float32
BF16 = mybir.dt.bfloat16
I32 = mybir.dt.int32
OP = mybir.AluOpType
AF = mybir.ActivationFunctionType
AX = mybir.AxisListType
bfdt = ml_dtypes.bfloat16

T = 2048
NT = 16
TH = 1024
HH = 8
NE = 32
NS = 16
D = 1024
ENGINES = ("sync", "vector", "scalar", "tensor", "gpsimd")


class Sched:
    def __init__(self):
        self.plan = {e: [] for e in ENGINES}
        self.counts = {}

    def sem(self, name):
        self.counts.setdefault(name, 0)
        return name

    def c(self, name):
        return self.counts.get(name, 0)

    def bump(self, name, k):
        self.sem(name)
        self.counts[name] += k

    def op(self, engine, fn, waits=(), inc=None, minc=None):
        rw = []
        for w in waits:
            s_, v = w if isinstance(w, tuple) else (w, self.counts.get(w, 0))
            self.sem(s_)
            if v > 0:
                rw.append((s_, v))
        if inc is not None:
            self.sem(inc[0])
        if minc is not None:
            self.sem(minc[0])
        self.plan[engine].append((fn, tuple(rw), inc, minc))
        if inc is not None:
            self.counts[inc[0]] += inc[1]
        if minc is not None:
            self.counts[minc[0]] += minc[1]

    def emit_fn(self, handles, engine, last):
        def body(eng):
            for fn, waits, inc, minc in self.plan[engine]:
                for s_, v in waits:
                    if last[engine].get(s_, 0) < v:
                        eng.wait_ge(handles[s_], v)
                        last[engine][s_] = v
                inst = fn(eng)
                if inc is not None:
                    assert inst is not None
                    inst.then_inc(handles[inc[0]], inc[1])
        return body


def build_program(do_ssm=True, do_ffn=True, n_ssm=8, do_y=True, do_nloop=True, return_sched=False, y_mode='par', races=True, dbg=False):
    nc = bass.Bass("TRN2", target_bir_lowering=False, detect_race_conditions=races)

    d_u = nc.dram_tensor("u", [T, D], F32, kind="ExternalInput").ap()
    d_wdt = nc.dram_tensor("wdt", [D, D], F32, kind="ExternalInput").ap()
    d_abc = nc.dram_tensor("abc", [128, 3 * 128], F32, kind="ExternalInput").ap()
    d_db = nc.dram_tensor("dpb", [128, 16], F32, kind="ExternalInput").ap()
    d_sel = nc.dram_tensor("sel", [128, 256], F32, kind="ExternalInput").ap()
    d_keys = nc.dram_tensor("keys", [NE, D, 512], F32, kind="ExternalInput").ap()
    d_vals = nc.dram_tensor("vals", [NE, 512, D], F32, kind="ExternalInput").ap()
    d_ident = nc.dram_tensor("ident", [128, 128], BF16, kind="ExternalInput").ap()
    d_cons = nc.dram_tensor("cons", [1, 40], I32, kind="ExternalInput").ap()
    d_mask = nc.dram_tensor("maskc", [128, 1], F32, kind="ExternalInput").ap()
    d_out = nc.dram_tensor("out", [T, D], F32, kind="ExternalOutput").ap()
    d_dbg = (nc.dram_tensor("dbg", [8 * 128, 1024], F32, kind="ExternalOutput").ap()
             if dbg else None)
    d_dbg2 = (nc.dram_tensor("dbg2", [128, 512], F32, kind="ExternalOutput").ap()
              if dbg else None)
    S_state = nc.dram_tensor("s_state", [16 * 128, NS], F32, addr_space="Shared")
    S_flag = nc.dram_tensor("s_flag", [1, 64], I32, addr_space="Shared")

    sb = nc.alloc_sbuf_tensor
    XNT = [sb(f"xnt{i}", [128, T], BF16) for i in range(HH)]
    DTT = [sb(f"dtt{i}", [128, T], BF16) for i in range(2)]
    ACC = [sb(f"acc{i}", [128, D], F32) for i in range(8)]
    KEYSB = [sb(f"keysb{i}", [128, 8 * 512], BF16) for i in range(2)]
    VALSB = [sb(f"valsb{i}", [128, 4 * 1024], BF16) for i in range(2)]
    AB = [sb(f"ab{i}", [128, T], F32) for i in range(2)]
    MPP = [sb(f"mpp{i}", [128, T], F32) for i in range(2)]
    T1B = [sb(f"t1b{i}", [128, T], F32) for i in range(2)]
    DU = sb("du", [128, T], BF16)
    YAC = sb("yac", [128, T], F32)
    XTK = sb("xtk", [128, T], BF16)
    WDTT = [sb(f"wdtt{i}", [128, 8 * 128], BF16) for i in range(2)]
    WDTB = [sb(f"wdtb{i}", [128, D], BF16) for i in range(2)]
    UROL = [sb(f"urol{i}", [128, D], F32) for i in range(2)]
    OST = sb("ost", [128, T], F32)
    GTS = sb("gts", [128, NT * 32], F32)
    SCT = [sb(f"sct{i}", [128, TH], BF16) for i in range(4)]
    YSCR4 = [sb(f"yscr{i}", [128, 512], BF16) for i in range(2)]
    PAR = sb("par", [128, 128 * 4], F32)
    SELB = sb("selb", [128, 256], BF16)
    DPB = sb("dpbs", [128, 16], F32)
    DMS = sb("dms", [128, 8], F32)
    IDENT = sb("identsb", [128, 128], BF16)
    LNC = sb("lnc", [128, 12], F32)
    GLG = sb("glg", [128, 96], F32)
    PUB = sb("pubt", [128, NS], F32)
    RCV = sb("rcvt", [128, NS], F32)
    ZT = sb("zt", [128, NS], F32)
    CONS = sb("conssb", [1, 40], I32)
    FLB = sb("flb", [1, 4], I32)
    MSK = sb("msk", [128, 1], F32)
    DMY = sb("dmy", [128, 2], F32)
    EPSC = sb("epsc", [128, 1], F32)
    ICOL = sb("icol", [128, 2], F32)
    ICOLS = sb("icols", [128, NS], F32)
    TK8 = PAR[:, 120:128]

    rem = nc.sbuf_bytes_remaining
    assert rem >= 0, f"SBUF over budget: remaining {rem}"

    PS_SC = [nc.alloc_psum_tensor(f"ps_sc{i}", [128, 1024], F32) for i in range(2)]
    PS_O = [nc.alloc_psum_tensor(f"ps_o{i}", [128, 1024], F32) for i in range(2)]
    PS_SC_B0 = PS_SC[0].ap().bitcast(BF16)
    PS_O_B0 = PS_O[0].ap().bitcast(BF16)

    s = Sched()
    sc_free = [None, None]
    o_free = [None, None]

    def dma(engine, out_ap, in_ap, sem, waits=()):
        s.op(engine, lambda e, o=out_ap, i=in_ap: e.dma_start(out=o, in_=i),
             waits=waits, inc=(sem, 16))

    dma("sync", CONS[:, :], d_cons, "lc")

    def preamble(e):
        e.reg_load(SR["en"], CONS[0:1, 32:33])
        e.reg_mov(SR["cnt"], 0)
        return None
    s.op("sync", preamble, waits=[("lc", 16)])
    dma("sync", MSK[:, :], d_mask, "lp")
    XTKF = XTK.ap().bitcast(F32)
    dma("sync", XTKF[:, 0:128], d_abc[:, 0:128], "lp")
    dma("sync", PAR[:, 0:128], d_abc[:, 128:256], "lp")
    dma("sync", PAR[:, 384:512], d_abc[:, 256:384], "lp")
    dma("sync", DPB[:, :], d_db, "lp")
    dma("sync", IDENT[:, :], d_ident, "lp")
    s.op("gpsimd", lambda e: e.dma_start(out=SELB[:, :], in_=d_sel), inc=("ls", 16))
    LP_ALL = s.c("lp")

    dma("scalar", UROL[0][:, :], d_u[0:128, :], "lu")
    dma("sync", UROL[1][:, :], d_u[128:256, :], "luo")

    def load_keys(sl):
        e_idx, par = sl % NE, sl % 2
        w = [("pe_mm1", min(4 * (sl - 1), s.c("pe_mm1")))] if sl >= 2 else ()
        s.op("gpsimd",
             lambda e, ei=e_idx, p=par: e.dma_start(
                 out=KEYSB[p].ap().rearrange("p (kc f) -> p kc f", kc=8),
                 in_=d_keys[ei].rearrange("(kc p) f -> p kc f", p=128)),
             waits=w, inc=("wk", 16))

    def load_vals(sl):
        e_idx, par = sl % NE, sl % 2
        w = [("pe_mm2", s.c("pe_mm2"))] if sl >= 1 else ()
        s.op("gpsimd",
             lambda e, ei=e_idx, p=par: e.dma_start(
                 out=VALSB[p].ap().rearrange("p (fc v) -> p fc v", fc=4),
                 in_=d_vals[ei].rearrange("(fc p) v -> p fc v", p=128)),
             waits=w, inc=("wv", 16))

    def load_wdt(j):
        w = [("pe_wx", s.c("pe_wx"))] if j >= 2 else ()
        s.op("gpsimd",
             lambda e, jj=j: e.dma_start(out=WDTB[jj % 2][:, :],
                                         in_=d_wdt[jj * 128:(jj + 1) * 128, :]),
             waits=w, inc=("lw", 16))

    if do_ffn:
        load_keys(0)
        load_vals(0)
        load_keys(1)
    if do_ssm:
        load_wdt(0)
        load_wdt(1)

    s.op("vector", lambda e: e.memset(ZT[:, :], 0.0), inc=("dvm", 1))
    s.op("vector", lambda e: e.memset(EPSC[:, :], 1e-5), inc=("dvm", 1))
    for hh in range(HH):
        def zslot(e, h=hh):
            with e.If_eq(SR["en"], 0):
                e.dma_start(out=S_state.ap()[h * 2 * 128:(h * 2 + 1) * 128, :],
                            in_=ZT[:, :]).then_inc(H["pub"], 16)
            with e.Else():
                e.dma_start(out=S_state.ap()[(h * 2 + 1) * 128:(h * 2 + 2) * 128, :],
                            in_=ZT[:, :]).then_inc(H["pub"], 16)
            return None
        s.op("sync", zslot, waits=["dvm"], minc=("pub", 16))

    A_log, At, Cc, Bt, Ct = (XTKF[:, 0:128], PAR[:, 128:256], PAR[:, 256:384],
                             PAR[:, 0:128], PAR[:, 384:512])
    s.op("scalar", lambda e: e.activation(At, A_log, AF.Exp), waits=[("lp", LP_ALL)], inc=("am", 1))
    s.op("vector", lambda e: e.tensor_scalar_mul(At, At, -1.0),
         waits=["am", ("dvm", s.c("dvm"))], inc=("dvm", 1))
    s.op("scalar", lambda e: e.activation(Cc, A_log, AF.Exp, scale=-1.0),
         waits=[("lp", LP_ALL)], inc=("am", 1))
    s.op("vector", lambda e: e.tensor_mul(Bt, Bt, Cc),
         waits=["am", ("dvm", s.c("dvm"))], inc=("dvm", 1))
    s.op("vector", lambda e: e.tensor_scalar_mul(Bt, Bt, -1.0),
         waits=[("dvm", s.c("dvm"))], inc=("dvm", 1))
    s.op("vector", lambda e: e.tensor_mul(Cc, Ct, Bt),
         waits=[("dvm", s.c("dvm"))], inc=("dvm", 1))

    for h in range(HH):
        s.op("vector", lambda e, hx=h: e.tensor_reduce(LNC[:, 0:1], Cc[:, hx * 16:(hx + 1) * 16],
                                                       AX.X, OP.add),
             waits=[("dvm", s.c("dvm"))], inc=("dvm", 1))
        s.op("vector", lambda e, hx=h: e.tensor_sub(DMS[:, hx:hx + 1], DPB[:, hx:hx + 1],
                                                    LNC[:, 0:1]),
             waits=[("dvm", s.c("dvm"))], inc=("dvm", 1))
    s.op("vector", lambda e: e.tensor_tensor_scan(DMY[:, 0:2], ZT[:, 0:2], ZT[:, 2:4], 0.0,
                                                  OP.mult, OP.add), inc=("dvm", 1))
    s.op("vector", lambda e: e.memset(DU[:, 0:1], 0.0), inc=("dvm", 1))
    PREP_DVM = s.c("dvm")

    invD = 1.0 / float(D)

    def emit_ln_stats(tt):
        p = (tt % 2) * 5
        ur = UROL[tt % 2]
        uw = ("lu", 16 * (tt // 2 + 1)) if tt % 2 == 0 else ("luo", 16 * (tt // 2 + 1))
        dw = [uw]
        if tt >= 2:
            dw.append(("dvx", s.c("dvx")))
        s.op("vector", lambda e, u=ur, c=p: e.tensor_reduce(LNC[:, c:c + 1], u[:, :], AX.X,
                                                            OP.add),
             waits=dw, inc=("dvm", 1))
        s.op("vector", lambda e, c=p: e.tensor_scalar(LNC[:, c + 1:c + 2], LNC[:, c:c + 1],
                                                      invD, -1.0, OP.mult, OP.mult),
             waits=[("dvm", s.c("dvm"))], inc=("dvm", 1))
        s.op("scalar", lambda e, u=ur, c=p: e.activation(PS_O[1].ap()[:, :], u[:, :], AF.Square,
                                                         bias=LNC[:, c + 1:c + 2],
                                                         accum_out=LNC[:, c + 2:c + 3]),
             waits=["dvm"], inc=("am", 1))
        s.op("scalar", lambda e, c=p: e.activation(LNC[:, c + 3:c + 4], LNC[:, c + 2:c + 3],
                                                   AF.Ln, bias=EPSC[:, 0:1], scale=invD),
             inc=("am", 1))
        s.op("scalar", lambda e, c=p: e.activation(LNC[:, c + 4:c + 5], LNC[:, c + 3:c + 4],
                                                   AF.Exp, scale=-0.5),
             inc=("am", 1))
        CH[("am", tt)] = s.c("am")

    def emit_ln_xn(tt):
        p = (tt % 2) * 5
        ur = UROL[tt % 2]
        xt = XTK[:, (tt % 2) * 1024:(tt % 2) * 1024 + 1024]
        xw = [("am", CH[("am", tt)])] + ([("dvm", PREP_DVM)] if tt == 0 else [])
        s.op("vector", lambda e, x=xt, u=ur, c=p: e.tensor_scalar(x, u[:, :],
                                                                  LNC[:, c + 1:c + 2],
                                                                  LNC[:, c + 4:c + 5],
                                                                  OP.add, OP.mult),
             waits=xw, inc=("dvx", 1))
        if tt + 2 < NT:
            dma("scalar" if tt % 2 == 0 else "sync", UROL[tt % 2][:, :],
                d_u[(tt + 2) * 128:(tt + 3) * 128, :], "lu" if tt % 2 == 0 else "luo",
                waits=[("dvx", s.c("dvx"))])

        def xp(e, x=xt):
            last = None
            for h in range(HH):
                last = e.transpose(PS_O_B0[:, h * 128:(h + 1) * 128],
                                   x[:, h * 128:(h + 1) * 128], IDENT[:, :])
            return last
        s.op("tensor", xp, waits=[("dvx", s.c("dvx")), ("lp", LP_ALL), ("dvv", s.c("dvv"))],
             inc=("pe_xp", 1))

        def xev(e, t=tt):
            last = None
            for h in range(HH):
                last = e.copy(XNT[h][:, t * 128:(t + 1) * 128],
                              PS_O_B0[:, h * 128:(h + 1) * 128])
            return last
        s.op("scalar", xev, waits=["pe_xp"], inc=("dvv", 1))

    def emit_ln(tt):
        if tt == 0:
            emit_ln_stats(0)
        if tt + 1 < NT:
            emit_ln_stats(tt + 1)
        emit_ln_xn(tt)

    GATE_DVT = {}

    def emit_gating(tt):
        def gmm(e, t=tt):
            last = None
            for k in range(HH):
                last = e.matmul(PS_SC[0].ap()[:, 0:32], XNT[k][:, t * 128:(t + 1) * 128],
                                SELB[:, k * 32:(k + 1) * 32],
                                start=(k == 0), stop=(k == HH - 1))
            return last
        gw = [("dvv", tt + 1), ("ag", s.c("ag")), ("lp", LP_ALL), ("ls", 16)]
        if tt >= 8:
            gw.append(("ae", 4))
        s.op("tensor", gmm, waits=gw, inc=("pe_g", 1))
        s.op("scalar", lambda e: e.copy(GLG[:, 0:32], PS_SC[0].ap()[:, 0:32]),
             waits=["pe_g", ("dvt", s.c("dvt"))], inc=("ag", 1))

        s.op("vector", lambda e: e.max(TK8[:, :], GLG[:, 0:32]), waits=["ag"], inc=("dvt", 1))
        s.op("vector", lambda e: e.tensor_single_scalar(GLG[:, 64:96], GLG[:, 0:32],
                                                        TK8[:, 3:4], OP.is_ge),
             waits=[("dvt", s.c("dvt"))], inc=("dvt", 1))
        s.op("scalar", lambda e: e.activation(GLG[:, 32:64], GLG[:, 0:32], AF.Exp, scale=-1.0),
             waits=["dvt"], inc=("ag", 1))

        s.op("vector", lambda e: e.tensor_scalar_add(GLG[:, 32:64], GLG[:, 32:64], 1.0),
             waits=["ag"], inc=("dvt", 1))
        s.op("vector", lambda e: e.reciprocal(GLG[:, 0:32], GLG[:, 32:64]),
             waits=[("dvt", s.c("dvt"))], inc=("dvt", 1))
        s.op("vector", lambda e, t=tt: e.tensor_mul(GTS[:, t * 32:(t + 1) * 32],
                                                    GLG[:, 0:32], GLG[:, 64:96]),
             waits=[("dvt", s.c("dvt"))], inc=("dvt", 1))
        GATE_DVT[tt] = s.c("dvt")

    def emit_wdt(j):
        def wxp(e, jj=j):
            last = None
            for k in range(HH):
                last = e.transpose(PS_SC_B0[:, k * 128:(k + 1) * 128],
                                   WDTB[jj % 2][:, k * 128:(k + 1) * 128], IDENT[:, :])
            return last
        waits = [("lw", 16 * (j + 1)), ("lp", LP_ALL)]
        if sc_free[0]:
            waits.append(sc_free[0])
        if sc_free[1]:
            waits.append(sc_free[1])
        s.op("tensor", wxp, waits=waits, inc=("pe_wx", 1))

        def wev(e, jj=j):
            last = None
            for k in range(HH):
                last = e.copy(WDTT[jj % 2][:, k * 128:(k + 1) * 128],
                              PS_SC_B0[:, k * 128:(k + 1) * 128])
            return last
        s.op("scalar", wev, waits=["pe_wx"], inc=("dvw", 1))

        def wmm(e, jj=j):
            last = None
            for sl_ in range(4):
                pss = PS_SC[sl_ // 2].ap()[:, (sl_ % 2) * 512:(sl_ % 2) * 512 + 512]
                for k in range(HH):
                    last = e.matmul(pss, WDTT[jj % 2][:, k * 128:(k + 1) * 128],
                                    XNT[k][:, sl_ * 512:(sl_ + 1) * 512],
                                    start=(k == 0), stop=(k == HH - 1))
            return last
        s.op("tensor", wmm, waits=["dvw", XNT_READY], inc=("pe_wdt", 1))

        def wevac(e, jj=j):
            last = None
            for sl_ in range(4):
                pss = PS_SC[sl_ // 2].ap()[:, (sl_ % 2) * 512:(sl_ % 2) * 512 + 512]
                last = e.activation(OST[:, sl_ * 512:(sl_ + 1) * 512], pss, AF.Exp,
                                    bias=DPB[:, 8 + jj:9 + jj])
            return last
        s.op("scalar", wevac, waits=["pe_wdt", ("dvc", s.c("dvc"))], inc=("ae", 1))
        sc_free[0] = ("ae", s.c("ae"))
        sc_free[1] = ("ae", s.c("ae"))
        w = ["ae"]
        if j >= 2:
            w.append(("ax", 16 * (j - 1)))
        s.op("vector", lambda e, jj=j: e.tensor_scalar(DTT[jj % 2][:, :], OST[:, :], 0.02, 0.001,
                                                       OP.min, OP.max), waits=w, inc=("dvc", 1))
        if j + 2 < HH:
            load_wdt(j + 2)

    CH = {}

    def emit_ssm_head(hh):
        CH['pub0'] = s.c("pub")
        pub0 = CH['pub0']
        s.op("gpsimd", lambda e, h=hh: e.tensor_sub(DU[:, 1:T], XNT[h][:, 1:T], XNT[h][:, 0:T - 1]),
             waits=[XNT_READY], inc=("dus", 1))

        s.op("vector", lambda e, h=hh: e.tensor_copy(ICOL[:, 0:1], XNT[h][:, 0:1]),
             waits=[("dvn", s.c("dvn"))], inc=("dvi", 1))
        CH['icol_dvi'] = s.c("dvi")
        for q_ in range(4):
            emit_ssm_q(hh, q_)

        s.op("vector", lambda e, h=hh: e.tensor_scalar_mul(YAC[:, :], XNT[h][:, :], DMS[:, h:h + 1]),
             waits=[("dvm", s.c("dvm")), ("dvyc", s.c("dvyc"))], inc=("dvs", 1))

    def emit_ssm_q(hh, q):
        fidx = hh * 4 + q
        def poll(e, h=hh, q_=q, f=fidx):
            e.reg_load(SR["tgt"], CONS[0:1, f:f + 1])
            e.reg_mov(SR["diff"], 0)
            e.reg_add(SR["diff"], SR["diff"], SR["en"])
            with e.While(SR["diff"]):
                e.reg_add(SR["cnt"], SR["cnt"], 16)
                e.dma_start(out=FLB[0:1, 0:1],
                            in_=S_flag.ap()[0:1, f * 2:f * 2 + 1]).then_inc(H["pl"], 16)
                e.wait_ge(H["pl"], SR["cnt"])
                e.reg_load(SR["flag"], FLB[0:1, 0:1])
                e.reg_sub(SR["diff"], SR["flag"], SR["tgt"])
            return e.dma_start(out=RCV[:, q_ * 4:q_ * 4 + 4],
                               in_=S_state.ap()[h * 2 * 128:(h * 2 + 1) * 128, q_ * 4:q_ * 4 + 4])
        s.op("sync", poll, waits=[("dvi", s.c("dvi")), ("pub", 16 * HH)], inc=("hs", 16))

    def emit_ssm_icols(hh, q):
        s.op("vector", lambda e, q_=q: e.tensor_scalar(
            ICOLS[:, q_ * 4:q_ * 4 + 4], RCV[:, q_ * 4:q_ * 4 + 4], MSK[:, 0:1],
            ICOL[:, 0:1], OP.mult, OP.add),
            waits=[("hs", 16 * (hh * 4 + q + 1)), ("dvi", CH['icol_dvi'])], inc=("dvi", 1))

    def emit_ssm_pubq(hh, q):
        fidx = hh * 4 + q
        def pubd(e, h=hh, q_=q):
            with e.If_eq(SR["en"], 0):
                e.dma_start(out=S_state.ap()[h * 2 * 128:(h * 2 + 1) * 128, q_ * 4:q_ * 4 + 4],
                            in_=PUB[:, q_ * 4:q_ * 4 + 4]).then_inc(H["pub"], 16)
            with e.Else():
                e.dma_start(out=S_state.ap()[(h * 2 + 1) * 128:(h * 2 + 2) * 128,
                                             q_ * 4:q_ * 4 + 4],
                            in_=PUB[:, q_ * 4:q_ * 4 + 4]).then_inc(H["pub"], 16)
            return None
        s.op("sync", pubd, waits=[("dvy", s.c("dvy"))], minc=("pub", 16))

        def pubf(e, f=fidx):
            with e.If_eq(SR["en"], 0):
                e.dma_start(out=S_flag.ap()[0:1, f * 2:f * 2 + 1],
                            in_=CONS[0:1, f:f + 1]).then_inc(H["pub"], 16)
            with e.Else():
                e.dma_start(out=S_flag.ap()[0:1, f * 2 + 1:f * 2 + 2],
                            in_=CONS[0:1, f:f + 1]).then_inc(H["pub"], 16)
            return None
        s.op("sync", pubf, waits=[("pub", s.c("pub"))], minc=("pub", 16))

    def emit_ssm_ns(hh, n):
        pub0 = CH['pub0']
        if n % 4 == 0:
            emit_ssm_icols(hh, n // 4)
        if True:
            idx = hh * 16 + n
            G = idx
            ab = AB[n % 2]
            mt = MPP[n % 2]
            m0, m1 = mt[:, 0:TH], mt[:, TH:T]
            adt = T1B[n % 2]
            ad0, ad1 = adt[:, 0:TH], adt[:, TH:T]
            ax_w = [("dvc", hh + 1)]
            if G >= 2:
                ax_w += [("pot", 2 * (G - 1)), ("dvn", 2 * (G - 1))]
            s.op("scalar", lambda e, h=hh, i=idx, a=ab: e.activation(
                a[:, :], DTT[h % 2][:, :], AF.Exp, scale=At[:, i:i + 1]), waits=ax_w, inc=("ax", 1))

            pw = [("ax", s.c("ax")), ("dus", s.c("dus"))]
            if G >= 2:
                pw.append(("dvn", 2 * G - 3))
            s.op("gpsimd", lambda e, a=ab, d0=ad0: e.tensor_mul(
                d0, a[:, 0:TH], DU[:, 0:TH]), waits=pw, inc=("pot", 1))
            pw2 = []
            if G >= 2:
                pw2.append(("dvn", 2 * G - 2))
            s.op("gpsimd", lambda e, a=ab, d1=ad1: e.tensor_mul(
                d1, a[:, TH:T], DU[:, TH:T]), waits=pw2, inc=("pot", 1))

            if CH.get('ya_pend'):
                CH['ya_pend'].pop(0)()
            sw = [("pot", s.c("pot") - 1), ("dvi", s.c("dvi"))]
            if G >= 2:
                sw += [("dvq", G - 1), ("dvy", G - 1)]
            s.op("vector", lambda e, a=ab, mm0=m0, d0=ad0, n_=n: e.tensor_tensor_scan(
                mm0, a[:, 0:TH], d0, ICOLS[:, n_:n_ + 1], OP.mult, OP.add),
                waits=sw, inc=("dvn", 1))
            sw2 = [("dvn", s.c("dvn")), ("pot", s.c("pot"))]
            if G >= 2:
                sw2.append(("dvy", G - 1))
            s.op("vector", lambda e, a=ab, m=mt, d1=ad1: e.tensor_tensor_scan(
                m[:, TH:T], a[:, TH:T], d1, m[:, TH - 1:TH],
                OP.mult, OP.add), waits=sw2, inc=("dvn", 1))

            yw = [("dvs", s.c("dvs"))] if n == 0 else []
            s.op("vector", lambda e, i=idx, m=mt: e.scalar_tensor_tensor(
                YAC[:, :], m[:, :], Cc[:, i:i + 1], YAC[:, :], OP.mult, OP.add),
                waits=yw, inc=("dvq", 1))
            s.op("vector", lambda e, h=hh, n_=n, m=mt: e.tensor_sub(
                PUB[:, n_:n_ + 1], m[:, T - 1:T], XNT[h][:, T - 1:T]),
                waits=[("dvn", 2 * G + 2), ("pub", pub0)], inc=("dvy", 1))
        if n % 4 == 3:
            emit_ssm_pubq(hh, n // 4)

    def emit_ssm_tail(hh):
        if not do_y:
            return
        YBF = XTK
        s.op("scalar", lambda e: e.copy(YBF[:, :], YAC[:, :]),
             waits=[("dvy", s.c("dvy")), ("ytr", s.c("ytr")), ("pe_xp", NT)], inc=("dvyc", 1))
        ya0 = s.c("ya")
        for g in range(4):
            sv = g % 2
            for k in range(4):
                tt = 4 * g + k
                src_b = YBF[:, tt * 128:(tt + 1) * 128]
                dst_b = YSCR4[sv][:, k * 128:(k + 1) * 128]
                tw = ["dvyc", ("ytr", max(0, s.c("ytr") - 32))]
                if k == 0 and g >= 2:
                    tw.append(("ya", ya0 + 16 * (g - 1)))
                s.op("sync", lambda e, d=dst_b, sr=src_b: e.dma_start_transpose(d, sr),
                     waits=tw, inc=("ytr", 16))
            yw = [("ytr", s.c("ytr")), ("sob", SOB_ALL)]
            if hh >= 5 and g < 2:
                yw.append(("soa", 128))

            def ya_emit(g_=g, h=hh, sv_=sv, w=tuple(yw)):
                s.op("gpsimd", lambda e: e.dma_start(
                    out=d_out[g_ * 512:(g_ + 1) * 512,
                              h * 128:(h + 1) * 128].rearrange("(b p) h -> p b h", p=128),
                    in_=YSCR4[sv_].ap().rearrange("p (b h) -> p b h", b=4),
                    accum_op=OP.add), waits=list(w), inc=("ya", 16))
            if hh == HH - 1:
                ya_emit()
            else:
                CH.setdefault('ya_pend', []).append(ya_emit)

    SCT2 = [UROL[i // 2].ap().bitcast(BF16)[:, (i % 2) * TH:(i % 2) * TH + TH]
            for i in range(4)]
    AE_DONE = {}
    LAST_SET_USE = {}

    def SCT_OF(sl):
        return SCT2 if (sl % 2 == 1 and sl >= 5) else SCT

    def emit_expert_mm1(sl):
        half, e_idx = sl // NE, sl % NE
        toff = half * TH
        par = sl % 2
        sct = SCT_OF(sl)
        for fc in range(4):
            st = fc % 2

            def mm1(e, f=fc, p=par, to=toff, stt=st):
                last = None
                for k in range(HH):
                    lhs = KEYSB[p][:, k * 512 + f * 128:k * 512 + f * 128 + 128]
                    for slab in range(2):
                        last = e.matmul(PS_SC[stt].ap()[:, slab * 512:slab * 512 + 512],
                                        lhs, XNT[k][:, to + slab * 512:to + slab * 512 + 512],
                                        start=(k == 0), stop=(k == HH - 1))
                return last
            waits = [("wk", 16 * (sl + 1)), ("dvv", 8 if sl < 32 else 16)]
            if sc_free[st]:
                waits.append(sc_free[st])
            if fc < 2 and sl >= 1:
                waits.append(("dvga", 8 * (sl - 1) + fc + 1))
            s.op("tensor", mm1, waits=waits, inc=("pe_mm1", 1))
            last_use = LAST_SET_USE.get(id(sct), -1)
            ew = ["pe_mm1", ("pe_mm2", 8 * (last_use + 1) if last_use >= 0 else 0)]
            if sct is SCT2 and last_use < 0:
                ew.append(("sob", SOB_ALL))
            s.op("scalar", lambda e, f=fc, stt=st, c=sct: e.activation(c[f][:, :],
                                                                       PS_SC[stt].ap()[:, :],
                                                                       AF.Relu),
                 waits=ew, inc=("ae", 1))
            sc_free[st] = ("ae", s.c("ae"))
        AE_DONE[sl] = s.c("ae")
        LAST_SET_USE[id(sct)] = sl

    def emit_expert_mm2(sl, ttls):
        half, e_idx = sl // NE, sl % NE
        par = sl % 2
        sct = SCT_OF(sl)
        for ttl in ttls:
            tt = half * 8 + ttl
            oset = ttl % 2
            pso = PS_SC[ttl] if ttl < 2 else PS_O[oset]

            def mm2(e, t=ttl, ps=pso, p=par, c=sct):
                last = None
                for fc in range(4):
                    lhs = c[fc][:, t * 128:(t + 1) * 128]
                    for vh in range(2):
                        last = e.matmul(ps.ap()[:, vh * 512:vh * 512 + 512], lhs,
                                        VALSB[p][:, fc * 1024 + vh * 512:fc * 1024 + vh * 512 + 512],
                                        start=(fc == 0), stop=(fc == 3))
                return last
            if ttl < 2:
                waits = [("ae", s.c("ae")), ("wv", 16 * (sl + 1))]
            else:
                waits = [("ae", AE_DONE[sl]), ("wv", 16 * (sl + 1))]
                if o_free[oset]:
                    waits.append(o_free[oset])
            s.op("tensor", mm2, waits=waits, inc=("pe_mm2", 1))
            gcol = GTS[:, tt * 32 + e_idx:tt * 32 + e_idx + 1]
            if half == 1 and e_idx == 0:
                s.op("vector", lambda e, t=ttl, ps=pso, g=gcol: e.tensor_single_scalar(
                    ACC[t][:, :], ps.ap()[:, :], g, OP.mult),
                    waits=["pe_mm2", ("dvt", GATE_DVT[half * 8 + ttl]),
                           ("soa", 16 * (ttl + 1))], inc=("dvga", 1))
            else:
                s.op("vector", lambda e, t=ttl, ps=pso, g=gcol: e.scalar_tensor_tensor(
                    ACC[t][:, :], ps.ap()[:, :], g, ACC[t][:, :], OP.mult, OP.add),
                    waits=["pe_mm2", ("dvt", GATE_DVT[half * 8 + ttl]), ("dva", 8)],
                    inc=("dvga", 1))
            if ttl >= 2:
                o_free[oset] = ("dvga", s.c("dvga"))

    def emit_accdma(half):
        ya_now = s.c("ya")
        for ttl in range(8):
            tt = half * 8 + ttl
            if dbg and half == 0:
                s.op("gpsimd", lambda e, j=ttl: e.dma_start(
                    out=d_dbg[j * 128:(j + 1) * 128, :], in_=ACC[j][:, :]),
                    waits=[("dvga", s.c("dvga"))], inc=("sdb", 16))
            s.op("gpsimd", lambda e, t=tt, j=ttl: e.dma_start(
                out=d_out[t * 128:(t + 1) * 128, :], in_=ACC[j][:, :], accum_op=OP.add),
                waits=[("dvga", s.c("dvga")), ("ya", ya_now), ("sob", SOB_ALL)],
                inc=("soa", 16))
        if dbg and half == 0:
            s.op("gpsimd", lambda e: e.dma_start(out=d_dbg2, in_=GTS[:, :]),
                 waits=[("dvt", s.c("dvt"))], inc=("sdb", 16))

    OLD_HEAD = {1: 0, 7: 1, 13: 2, 19: 3, 25: 4, 32: 5, 38: 6, 44: 7}
    HEAD_AT = {sl - 1: hh for sl, hh in OLD_HEAD.items()}
    WDT_AT = {7: 2, 13: 3, 19: 4, 25: 5, 31: 6, 38: 7}
    NS_AT = {}
    TAIL_AT = {}
    for sl0, hh0 in OLD_HEAD.items():
        batches = [[0, 1], [2, 3, 4], [5, 6, 7], [8, 9, 10], [11, 12, 13], [14, 15]]
        for k, b in enumerate(batches):
            NS_AT.setdefault(sl0 + k, []).append((hh0, b))
        TAIL_AT[sl0 + 5] = hh0
    for tt in range(8):
        emit_ln(tt)
    if do_ffn:
        for tt in range(8):
            emit_gating(tt)
        sc_free[0] = ("ag", s.c("ag"))
        sc_free[1] = None
        emit_expert_mm1(0)
    for tt in range(8, NT):
        emit_ln(tt)
    for tt in range(NT):
        s.op("scalar", lambda e, t=tt: e.dma_start(out=d_out[t * 128:(t + 1) * 128, :],
                                                   in_=d_u[t * 128:(t + 1) * 128, :]),
             inc=("sob", 16))
    XNT_READY = ("dvv", s.c("dvv"))
    SOB_ALL = s.c("sob")
    o_free[0] = ("dvv", s.c("dvv"))
    o_free[1] = ("am", s.c("am"))
    if do_ffn:
        for tt in range(8, NT):
            emit_gating(tt)
        sc_free[0] = ("ag", s.c("ag"))
    for i in range(8):
        s.op("vector", lambda e, j=i: e.memset(ACC[j][:, :], 0.0), inc=("dva", 1))
    if do_ssm:
        emit_wdt(0)
        emit_wdt(1)
    for sl in range(64):
        if do_ffn:
            if sl + 2 < 64:
                load_keys(sl + 2)
            if sl + 1 < 64:
                load_vals(sl + 1)
        if do_ssm and sl in WDT_AT:
            emit_wdt(WDT_AT[sl])
        pieces = []
        if do_ssm:
            for hh0, b in NS_AT.get(sl, []):
                for n0 in b:
                    pieces.append((hh0, n0))
        interleave = sl + 1 >= 6
        if do_ffn:
            emit_expert_mm2(sl, [0, 1])
        for hh0, n0 in pieces[:2]:
            emit_ssm_ns(hh0, n0)
        if do_ffn:
            if sl + 1 < 64 and interleave:
                emit_expert_mm1(sl + 1)
            emit_expert_mm2(sl, [2, 3, 4, 5])
        for hh0, n0 in pieces[2:]:
            emit_ssm_ns(hh0, n0)
        if do_ffn:
            emit_expert_mm2(sl, [6, 7])
            if sl + 1 < 64 and not interleave:
                emit_expert_mm1(sl + 1)
        if do_ssm and sl in TAIL_AT:
            emit_ssm_tail(TAIL_AT[sl])
        if do_ssm and sl in HEAD_AT:
            emit_ssm_head(HEAD_AT[sl])
        if do_ffn and sl == 31:
            emit_accdma(0)
    if do_ffn:
        emit_accdma(1)

    s.op("sync", lambda e: e.nop(),
         waits=[("sob", s.c("sob")), ("soa", s.c("soa")), ("ya", s.c("ya")),
                ("sdb", s.c("sdb")),
                ("pub", s.c("pub")), ("hs", s.c("hs")),
                ("wk", s.c("wk")), ("wv", s.c("wv"))])

    H = {}
    for name in list(s.counts) + ["pl"]:
        H[name] = nc.alloc_semaphore(name)
    SR = {}
    last = {e: {} for e in ENGINES}

    if return_sched:
        return s
    with nc.Block() as block:
        def sync_body(eng):
            SR["en"] = eng.alloc_register("r_en")
            SR["cnt"] = eng.alloc_register("r_cnt")
            SR["flag"] = eng.alloc_register("r_flag")
            SR["tgt"] = eng.alloc_register("r_tgt")
            SR["diff"] = eng.alloc_register("r_diff")
            s.emit_fn(H, "sync", last)(eng)
        block.sync(sync_body)
        block.vector(s.emit_fn(H, "vector", last))
        block.scalar(s.emit_fn(H, "scalar", last))
        block.tensor(s.emit_fn(H, "tensor", last))
        block.gpsimd(s.emit_fn(H, "gpsimd", last))

    return nc


_NC_CACHE = {}


def kernel(L0_attn_u, L0_attn_A_log, L0_attn_B_p, L0_attn_C_p, L0_attn_D_p,
           L0_attn_W_dt, L0_attn_b_dt, L0_ffn_expert_sel, L0_ffn_keys, L0_ffn_values):
    u = np.ascontiguousarray(np.asarray(L0_attn_u, dtype=np.float32))
    A_log = np.asarray(L0_attn_A_log, np.float32)
    B_p = np.asarray(L0_attn_B_p, np.float32)
    C_p = np.asarray(L0_attn_C_p, np.float32)
    D_p = np.asarray(L0_attn_D_p, np.float32)
    W_dt = np.ascontiguousarray(np.asarray(L0_attn_W_dt, np.float32))
    b_dt = np.asarray(L0_attn_b_dt, np.float32)
    sel = np.asarray(L0_ffn_expert_sel, np.float32)
    keys = np.ascontiguousarray(np.asarray(L0_ffn_keys, np.float32))
    vals = np.ascontiguousarray(np.asarray(L0_ffn_values, np.float32))

    def chunkp(x):
        n = x.shape[1]
        return np.ascontiguousarray(x.reshape(HH, 128, n).transpose(1, 0, 2).reshape(128, HH * n))

    abc = np.concatenate([chunkp(A_log), chunkp(B_p), chunkp(C_p)], axis=1)
    dpb = np.concatenate([np.ascontiguousarray(D_p.reshape(HH, 128).T),
                          np.ascontiguousarray(b_dt.reshape(HH, 128).T)], axis=1)
    selr = np.ascontiguousarray(sel.T.reshape(HH, 128, 32).transpose(1, 0, 2).reshape(128, 256))
    ident = np.eye(128, dtype=np.float32).astype(bfdt)

    nonce = (int.from_bytes(os.urandom(4), "little") & 0x3FFFFFF0) + 16
    if "nc" not in _NC_CACHE:
        _NC_CACHE["nc"] = build_program()
    nc = _NC_CACHE["nc"]

    in_maps = []
    for c in range(8):
        b, half = c // 2, c % 2
        cons = np.zeros((1, 40), np.int32)
        for f in range(4 * HH):
            cons[0, f] = nonce + f
        cons[0, 32] = 1 if half == 1 else 0
        mask = np.full((128, 1), 1.0 if half == 1 else 0.0, np.float32)
        in_maps.append({
            "u": np.ascontiguousarray(u[b, half * T:(half + 1) * T, :]),
            "wdt": W_dt, "abc": abc, "dpb": dpb, "sel": selr,
            "keys": keys, "vals": vals, "ident": ident,
            "cons": cons, "maskc": mask,
        })
    res = run_bass_kernel_spmd(nc, in_maps, core_ids=list(range(8)))
    out = np.empty((4, 4096, 1024), np.float32)
    for c in range(8):
        b, half = c // 2, c % 2
        out[b, half * T:(half + 1) * T, :] = res.results[c]["out"]
    return out


# version 38 (final)
# speedup vs baseline: 1.0568x; 1.0568x over previous
import os
import numpy as np
import ml_dtypes
import concourse.bass as bass
import concourse.mybir as mybir
from concourse.bass_utils import run_bass_kernel_spmd

F32 = mybir.dt.float32
BF16 = mybir.dt.bfloat16
I32 = mybir.dt.int32
OP = mybir.AluOpType
AF = mybir.ActivationFunctionType
AX = mybir.AxisListType
bfdt = ml_dtypes.bfloat16

T = 2048
NT = 16
TH = 1024
HH = 8
NE = 32
NS = 16
D = 1024
ENGINES = ("sync", "vector", "scalar", "tensor", "gpsimd")


class Sched:
    def __init__(self):
        self.plan = {e: [] for e in ENGINES}
        self.counts = {}

    def sem(self, name):
        self.counts.setdefault(name, 0)
        return name

    def c(self, name):
        return self.counts.get(name, 0)

    def bump(self, name, k):
        self.sem(name)
        self.counts[name] += k

    def op(self, engine, fn, waits=(), inc=None, minc=None):
        rw = []
        for w in waits:
            s_, v = w if isinstance(w, tuple) else (w, self.counts.get(w, 0))
            self.sem(s_)
            if v > 0:
                rw.append((s_, v))
        if inc is not None:
            self.sem(inc[0])
        if minc is not None:
            self.sem(minc[0])
        self.plan[engine].append((fn, tuple(rw), inc, minc))
        if inc is not None:
            self.counts[inc[0]] += inc[1]
        if minc is not None:
            self.counts[minc[0]] += minc[1]

    def emit_fn(self, handles, engine, last):
        def body(eng):
            for fn, waits, inc, minc in self.plan[engine]:
                for s_, v in waits:
                    if last[engine].get(s_, 0) < v:
                        eng.wait_ge(handles[s_], v)
                        last[engine][s_] = v
                inst = fn(eng)
                if inc is not None:
                    assert inst is not None
                    inst.then_inc(handles[inc[0]], inc[1])
        return body


def build_program(do_ssm=True, do_ffn=True, n_ssm=8, do_y=True, do_nloop=True, return_sched=False, y_mode='par', races=True, dbg=False):
    nc = bass.Bass("TRN2", target_bir_lowering=False, detect_race_conditions=races)

    d_u = nc.dram_tensor("u", [T, D], F32, kind="ExternalInput").ap()
    d_wdt = nc.dram_tensor("wdt", [D, D], F32, kind="ExternalInput").ap()
    d_abc = nc.dram_tensor("abc", [128, 3 * 128], F32, kind="ExternalInput").ap()
    d_db = nc.dram_tensor("dpb", [128, 16], F32, kind="ExternalInput").ap()
    d_sel = nc.dram_tensor("sel", [128, 256], F32, kind="ExternalInput").ap()
    d_keys = nc.dram_tensor("keys", [NE, D, 512], F32, kind="ExternalInput").ap()
    d_vals = nc.dram_tensor("vals", [NE, 512, D], F32, kind="ExternalInput").ap()
    d_ident = nc.dram_tensor("ident", [128, 128], BF16, kind="ExternalInput").ap()
    d_cons = nc.dram_tensor("cons", [1, 40], I32, kind="ExternalInput").ap()
    d_mask = nc.dram_tensor("maskc", [128, 1], F32, kind="ExternalInput").ap()
    d_out = nc.dram_tensor("out", [T, D], F32, kind="ExternalOutput").ap()
    d_dbg = (nc.dram_tensor("dbg", [8 * 128, 1024], F32, kind="ExternalOutput").ap()
             if dbg else None)
    d_dbg2 = (nc.dram_tensor("dbg2", [128, 512], F32, kind="ExternalOutput").ap()
              if dbg else None)
    S_state = nc.dram_tensor("s_state", [16 * 128, NS], F32, addr_space="Shared")
    S_flag = nc.dram_tensor("s_flag", [1, 64], I32, addr_space="Shared")

    sb = nc.alloc_sbuf_tensor
    XNT = [sb(f"xnt{i}", [128, T], BF16) for i in range(HH)]
    DTT = [sb(f"dtt{i}", [128, T], BF16) for i in range(2)]
    ACC = [sb(f"acc{i}", [128, D], F32) for i in range(8)]
    KEYSB = [sb(f"keysb{i}", [128, 8 * 512], BF16) for i in range(2)]
    VALSB = [sb(f"valsb{i}", [128, 4 * 1024], BF16) for i in range(2)]
    AB = [sb(f"ab{i}", [128, T], F32) for i in range(2)]
    MPP = [sb(f"mpp{i}", [128, T], F32) for i in range(2)]
    T1B = [sb(f"t1b{i}", [128, T], F32) for i in range(2)]
    DU = sb("du", [128, T], BF16)
    YAC = sb("yac", [128, T], F32)
    XTK = sb("xtk", [128, T], BF16)
    WDTT = [sb(f"wdtt{i}", [128, 8 * 128], BF16) for i in range(2)]
    WDTB = [sb(f"wdtb{i}", [128, D], BF16) for i in range(2)]
    UROL = [sb(f"urol{i}", [128, D], F32) for i in range(2)]
    OST = sb("ost", [128, T], F32)
    GTS = sb("gts", [128, NT * 32], F32)
    SCT = [sb(f"sct{i}", [128, TH], BF16) for i in range(4)]
    YSCR4 = [sb(f"yscr{i}", [128, 512], BF16) for i in range(2)]
    PAR = sb("par", [128, 128 * 4], F32)
    SELB = sb("selb", [128, 256], BF16)
    DPB = sb("dpbs", [128, 16], F32)
    DMS = sb("dms", [128, 8], F32)
    IDENT = sb("identsb", [128, 128], BF16)
    LNC = sb("lnc", [128, 12], F32)
    GLG = sb("glg", [128, 96], F32)
    PUB = sb("pubt", [128, NS], F32)
    RCV = sb("rcvt", [128, NS], F32)
    ZT = sb("zt", [128, NS], F32)
    CONS = sb("conssb", [1, 40], I32)
    FLB = sb("flb", [1, 4], I32)
    MSK = sb("msk", [128, 1], F32)
    DMY = sb("dmy", [128, 2], F32)
    EPSC = sb("epsc", [128, 1], F32)
    ICOL = sb("icol", [128, 2], F32)
    ICOLS = sb("icols", [128, NS], F32)
    TK8 = PAR[:, 120:128]

    rem = nc.sbuf_bytes_remaining
    assert rem >= 0, f"SBUF over budget: remaining {rem}"

    PS_SC = [nc.alloc_psum_tensor(f"ps_sc{i}", [128, 1024], F32) for i in range(2)]
    PS_O = [nc.alloc_psum_tensor(f"ps_o{i}", [128, 1024], F32) for i in range(2)]
    PS_SC_B0 = PS_SC[0].ap().bitcast(BF16)
    PS_O_B0 = PS_O[0].ap().bitcast(BF16)

    s = Sched()
    sc_free = [None, None]
    o_free = [None, None]

    def dma(engine, out_ap, in_ap, sem, waits=()):
        s.op(engine, lambda e, o=out_ap, i=in_ap: e.dma_start(out=o, in_=i),
             waits=waits, inc=(sem, 16))

    dma("sync", CONS[:, :], d_cons, "lc")

    def preamble(e):
        e.reg_load(SR["en"], CONS[0:1, 32:33])
        e.reg_mov(SR["cnt"], 0)
        return None
    s.op("sync", preamble, waits=[("lc", 16)])
    dma("sync", MSK[:, :], d_mask, "lp")
    XTKF = XTK.ap().bitcast(F32)
    dma("sync", XTKF[:, 0:128], d_abc[:, 0:128], "lp")
    dma("sync", PAR[:, 0:128], d_abc[:, 128:256], "lp")
    dma("sync", PAR[:, 384:512], d_abc[:, 256:384], "lp")
    dma("sync", DPB[:, :], d_db, "lp")
    dma("sync", IDENT[:, :], d_ident, "lp")
    s.op("gpsimd", lambda e: e.dma_start(out=SELB[:, :], in_=d_sel), inc=("ls", 16))
    LP_ALL = s.c("lp")

    dma("scalar", UROL[0][:, :], d_u[0:128, :], "lu")
    dma("sync", UROL[1][:, :], d_u[128:256, :], "luo")

    def load_keys(sl):
        e_idx, par = sl % NE, sl % 2
        w = [("pe_mm1", min(4 * (sl - 1), s.c("pe_mm1")))] if sl >= 2 else ()
        s.op("gpsimd",
             lambda e, ei=e_idx, p=par: e.dma_start(
                 out=KEYSB[p].ap().rearrange("p (kc f) -> p kc f", kc=8),
                 in_=d_keys[ei].rearrange("(kc p) f -> p kc f", p=128)),
             waits=w, inc=("wk", 16))

    def load_vals(sl):
        e_idx, par = sl % NE, sl % 2
        w = [("pe_mm2", s.c("pe_mm2"))] if sl >= 1 else ()
        s.op("gpsimd",
             lambda e, ei=e_idx, p=par: e.dma_start(
                 out=VALSB[p].ap().rearrange("p (fc v) -> p fc v", fc=4),
                 in_=d_vals[ei].rearrange("(fc p) v -> p fc v", p=128)),
             waits=w, inc=("wv", 16))

    def load_wdt(j):
        w = [("pe_wx", s.c("pe_wx"))] if j >= 2 else ()
        s.op("gpsimd",
             lambda e, jj=j: e.dma_start(out=WDTB[jj % 2][:, :],
                                         in_=d_wdt[jj * 128:(jj + 1) * 128, :]),
             waits=w, inc=("lw", 16))

    if do_ffn:
        load_keys(0)
        load_vals(0)
        load_keys(1)
    if do_ssm:
        load_wdt(0)
        load_wdt(1)

    s.op("vector", lambda e: e.memset(ZT[:, :], 0.0), inc=("dvm", 1))
    s.op("vector", lambda e: e.memset(EPSC[:, :], 1e-5), inc=("dvm", 1))
    for hh in range(HH):
        def zslot(e, h=hh):
            with e.If_eq(SR["en"], 0):
                e.dma_start(out=S_state.ap()[h * 2 * 128:(h * 2 + 1) * 128, :],
                            in_=ZT[:, :]).then_inc(H["pub"], 16)
            with e.Else():
                e.dma_start(out=S_state.ap()[(h * 2 + 1) * 128:(h * 2 + 2) * 128, :],
                            in_=ZT[:, :]).then_inc(H["pub"], 16)
            return None
        s.op("sync", zslot, waits=["dvm"], minc=("pub", 16))

    A_log, At, Cc, Bt, Ct = (XTKF[:, 0:128], PAR[:, 128:256], PAR[:, 256:384],
                             PAR[:, 0:128], PAR[:, 384:512])
    s.op("scalar", lambda e: e.activation(At, A_log, AF.Exp), waits=[("lp", LP_ALL)], inc=("am", 1))
    s.op("vector", lambda e: e.tensor_scalar_mul(At, At, -1.0),
         waits=["am", ("dvm", s.c("dvm"))], inc=("dvm", 1))
    s.op("scalar", lambda e: e.activation(Cc, A_log, AF.Exp, scale=-1.0),
         waits=[("lp", LP_ALL)], inc=("am", 1))
    s.op("vector", lambda e: e.tensor_mul(Bt, Bt, Cc),
         waits=["am", ("dvm", s.c("dvm"))], inc=("dvm", 1))
    s.op("vector", lambda e: e.tensor_scalar_mul(Bt, Bt, -1.0),
         waits=[("dvm", s.c("dvm"))], inc=("dvm", 1))
    s.op("vector", lambda e: e.tensor_mul(Cc, Ct, Bt),
         waits=[("dvm", s.c("dvm"))], inc=("dvm", 1))

    for h in range(HH):
        s.op("vector", lambda e, hx=h: e.tensor_reduce(LNC[:, 0:1], Cc[:, hx * 16:(hx + 1) * 16],
                                                       AX.X, OP.add),
             waits=[("dvm", s.c("dvm"))], inc=("dvm", 1))
        s.op("vector", lambda e, hx=h: e.tensor_sub(DMS[:, hx:hx + 1], DPB[:, hx:hx + 1],
                                                    LNC[:, 0:1]),
             waits=[("dvm", s.c("dvm"))], inc=("dvm", 1))
    s.op("vector", lambda e: e.tensor_tensor_scan(DMY[:, 0:2], ZT[:, 0:2], ZT[:, 2:4], 0.0,
                                                  OP.mult, OP.add), inc=("dvm", 1))
    s.op("vector", lambda e: e.memset(DU[:, 0:1], 0.0), inc=("dvm", 1))
    PREP_DVM = s.c("dvm")

    invD = 1.0 / float(D)

    def emit_ln_stats(tt):
        p = (tt % 2) * 5
        ur = UROL[tt % 2]
        uw = ("lu", 16 * (tt // 2 + 1)) if tt % 2 == 0 else ("luo", 16 * (tt // 2 + 1))
        dw = [uw]
        if tt >= 2:
            dw.append(("dvx", s.c("dvx")))
        s.op("vector", lambda e, u=ur, c=p: e.tensor_reduce(LNC[:, c:c + 1], u[:, :], AX.X,
                                                            OP.add),
             waits=dw, inc=("dvm", 1))
        s.op("vector", lambda e, c=p: e.tensor_scalar(LNC[:, c + 1:c + 2], LNC[:, c:c + 1],
                                                      invD, -1.0, OP.mult, OP.mult),
             waits=[("dvm", s.c("dvm"))], inc=("dvm", 1))
        s.op("scalar", lambda e, u=ur, c=p: e.activation(PS_O[1].ap()[:, :], u[:, :], AF.Square,
                                                         bias=LNC[:, c + 1:c + 2],
                                                         accum_out=LNC[:, c + 2:c + 3]),
             waits=["dvm"], inc=("am", 1))
        s.op("scalar", lambda e, c=p: e.activation(LNC[:, c + 3:c + 4], LNC[:, c + 2:c + 3],
                                                   AF.Ln, bias=EPSC[:, 0:1], scale=invD),
             inc=("am", 1))
        s.op("scalar", lambda e, c=p: e.activation(LNC[:, c + 4:c + 5], LNC[:, c + 3:c + 4],
                                                   AF.Exp, scale=-0.5),
             inc=("am", 1))
        CH[("am", tt)] = s.c("am")

    def emit_ln_xn(tt):
        p = (tt % 2) * 5
        ur = UROL[tt % 2]
        xt = XTK[:, (tt % 2) * 1024:(tt % 2) * 1024 + 1024]
        xw = [("am", CH[("am", tt)])] + ([("dvm", PREP_DVM)] if tt == 0 else [])
        s.op("vector", lambda e, x=xt, u=ur, c=p: e.tensor_scalar(x, u[:, :],
                                                                  LNC[:, c + 1:c + 2],
                                                                  LNC[:, c + 4:c + 5],
                                                                  OP.add, OP.mult),
             waits=xw, inc=("dvx", 1))
        if tt + 2 < NT:
            dma("scalar" if tt % 2 == 0 else "sync", UROL[tt % 2][:, :],
                d_u[(tt + 2) * 128:(tt + 3) * 128, :], "lu" if tt % 2 == 0 else "luo",
                waits=[("dvx", s.c("dvx"))])

        def xp(e, x=xt):
            last = None
            for h in range(HH):
                last = e.transpose(PS_O_B0[:, h * 128:(h + 1) * 128],
                                   x[:, h * 128:(h + 1) * 128], IDENT[:, :])
            return last
        s.op("tensor", xp, waits=[("dvx", s.c("dvx")), ("lp", LP_ALL), ("dvv", s.c("dvv"))],
             inc=("pe_xp", 1))

        def xev(e, t=tt):
            last = None
            for h in range(HH):
                last = e.copy(XNT[h][:, t * 128:(t + 1) * 128],
                              PS_O_B0[:, h * 128:(h + 1) * 128])
            return last
        s.op("scalar", xev, waits=["pe_xp"], inc=("dvv", 1))

    def emit_ln(tt):
        if tt == 0:
            emit_ln_stats(0)
        if tt + 1 < NT:
            emit_ln_stats(tt + 1)
        emit_ln_xn(tt)

    GATE_DVT = {}

    def emit_gating(tt):
        def gmm(e, t=tt):
            last = None
            for k in range(HH):
                last = e.matmul(PS_SC[0].ap()[:, 0:32], XNT[k][:, t * 128:(t + 1) * 128],
                                SELB[:, k * 32:(k + 1) * 32],
                                start=(k == 0), stop=(k == HH - 1))
            return last
        gw = [("dvv", tt + 1), ("ag", s.c("ag")), ("lp", LP_ALL), ("ls", 16)]
        if tt >= 8:
            gw.append(("ae", 4))
        s.op("tensor", gmm, waits=gw, inc=("pe_g", 1))
        s.op("scalar", lambda e: e.copy(GLG[:, 0:32], PS_SC[0].ap()[:, 0:32]),
             waits=["pe_g", ("dvt", s.c("dvt"))], inc=("ag", 1))

        s.op("vector", lambda e: e.max(TK8[:, :], GLG[:, 0:32]), waits=["ag"], inc=("dvt", 1))
        s.op("vector", lambda e: e.tensor_single_scalar(GLG[:, 64:96], GLG[:, 0:32],
                                                        TK8[:, 3:4], OP.is_ge),
             waits=[("dvt", s.c("dvt"))], inc=("dvt", 1))
        s.op("scalar", lambda e: e.activation(GLG[:, 32:64], GLG[:, 0:32], AF.Exp, scale=-1.0),
             waits=["dvt"], inc=("ag", 1))

        s.op("vector", lambda e: e.tensor_scalar_add(GLG[:, 32:64], GLG[:, 32:64], 1.0),
             waits=["ag"], inc=("dvt", 1))
        s.op("vector", lambda e: e.reciprocal(GLG[:, 0:32], GLG[:, 32:64]),
             waits=[("dvt", s.c("dvt"))], inc=("dvt", 1))
        s.op("vector", lambda e, t=tt: e.tensor_mul(GTS[:, t * 32:(t + 1) * 32],
                                                    GLG[:, 0:32], GLG[:, 64:96]),
             waits=[("dvt", s.c("dvt"))], inc=("dvt", 1))
        GATE_DVT[tt] = s.c("dvt")

    def emit_wdt(j):
        def wxp(e, jj=j):
            last = None
            for k in range(HH):
                last = e.transpose(PS_SC_B0[:, k * 128:(k + 1) * 128],
                                   WDTB[jj % 2][:, k * 128:(k + 1) * 128], IDENT[:, :])
            return last
        waits = [("lw", 16 * (j + 1)), ("lp", LP_ALL)]
        if sc_free[0]:
            waits.append(sc_free[0])
        if sc_free[1]:
            waits.append(sc_free[1])
        s.op("tensor", wxp, waits=waits, inc=("pe_wx", 1))

        def wev(e, jj=j):
            last = None
            for k in range(HH):
                last = e.copy(WDTT[jj % 2][:, k * 128:(k + 1) * 128],
                              PS_SC_B0[:, k * 128:(k + 1) * 128])
            return last
        s.op("scalar", wev, waits=["pe_wx"], inc=("dvw", 1))

        def wmm(e, jj=j):
            last = None
            for sl_ in range(4):
                pss = PS_SC[sl_ // 2].ap()[:, (sl_ % 2) * 512:(sl_ % 2) * 512 + 512]
                for k in range(HH):
                    last = e.matmul(pss, WDTT[jj % 2][:, k * 128:(k + 1) * 128],
                                    XNT[k][:, sl_ * 512:(sl_ + 1) * 512],
                                    start=(k == 0), stop=(k == HH - 1))
            return last
        s.op("tensor", wmm, waits=["dvw", XNT_READY], inc=("pe_wdt", 1))

        def wevac(e, jj=j):
            last = None
            for sl_ in range(4):
                pss = PS_SC[sl_ // 2].ap()[:, (sl_ % 2) * 512:(sl_ % 2) * 512 + 512]
                last = e.activation(OST[:, sl_ * 512:(sl_ + 1) * 512], pss, AF.Exp,
                                    bias=DPB[:, 8 + jj:9 + jj])
            return last
        s.op("scalar", wevac, waits=["pe_wdt", ("dvc", s.c("dvc"))], inc=("ae", 1))
        sc_free[0] = ("ae", s.c("ae"))
        sc_free[1] = ("ae", s.c("ae"))
        w = ["ae"]
        if j >= 2:
            w.append(("ax", 16 * (j - 1)))
        s.op("vector", lambda e, jj=j: e.tensor_scalar(DTT[jj % 2][:, :], OST[:, :], 0.02, 0.001,
                                                       OP.min, OP.max), waits=w, inc=("dvc", 1))
        if j + 2 < HH:
            load_wdt(j + 2)

    CH = {}

    def emit_ssm_head(hh):
        CH['pub0'] = s.c("pub")
        pub0 = CH['pub0']
        s.op("gpsimd", lambda e, h=hh: e.tensor_sub(DU[:, 1:T], XNT[h][:, 1:T], XNT[h][:, 0:T - 1]),
             waits=[XNT_READY], inc=("dus", 1))

        s.op("vector", lambda e, h=hh: e.tensor_copy(ICOL[:, 0:1], XNT[h][:, 0:1]),
             waits=[("dvn", s.c("dvn"))], inc=("dvi", 1))
        CH['icol_dvi'] = s.c("dvi")
        for q_ in range(4):
            emit_ssm_q(hh, q_)

        s.op("vector", lambda e, h=hh: e.tensor_scalar_mul(YAC[:, :], XNT[h][:, :], DMS[:, h:h + 1]),
             waits=[("dvm", s.c("dvm")), ("dvyc", s.c("dvyc"))], inc=("dvs", 1))

    def emit_ssm_q(hh, q):
        fidx = hh * 4 + q
        def poll(e, h=hh, q_=q, f=fidx):
            e.reg_load(SR["tgt"], CONS[0:1, f:f + 1])
            e.reg_mov(SR["diff"], 0)
            e.reg_add(SR["diff"], SR["diff"], SR["en"])
            with e.While(SR["diff"]):
                e.reg_add(SR["cnt"], SR["cnt"], 16)
                e.dma_start(out=FLB[0:1, 0:1],
                            in_=S_flag.ap()[0:1, f * 2:f * 2 + 1]).then_inc(H["pl"], 16)
                e.wait_ge(H["pl"], SR["cnt"])
                e.reg_load(SR["flag"], FLB[0:1, 0:1])
                e.reg_sub(SR["diff"], SR["flag"], SR["tgt"])
            return e.dma_start(out=RCV[:, q_ * 4:q_ * 4 + 4],
                               in_=S_state.ap()[h * 2 * 128:(h * 2 + 1) * 128, q_ * 4:q_ * 4 + 4])
        s.op("sync", poll, waits=[("dvi", s.c("dvi")), ("pub", 16 * HH)], inc=("hs", 16))

    def emit_ssm_icols(hh, q):
        s.op("vector", lambda e, q_=q: e.tensor_scalar(
            ICOLS[:, q_ * 4:q_ * 4 + 4], RCV[:, q_ * 4:q_ * 4 + 4], MSK[:, 0:1],
            ICOL[:, 0:1], OP.mult, OP.add),
            waits=[("hs", 16 * (hh * 4 + q + 1)), ("dvi", CH['icol_dvi'])], inc=("dvi", 1))

    def emit_ssm_pubq(hh, q):
        fidx = hh * 4 + q
        def pubd(e, h=hh, q_=q):
            with e.If_eq(SR["en"], 0):
                e.dma_start(out=S_state.ap()[h * 2 * 128:(h * 2 + 1) * 128, q_ * 4:q_ * 4 + 4],
                            in_=PUB[:, q_ * 4:q_ * 4 + 4]).then_inc(H["pub"], 16)
            with e.Else():
                e.dma_start(out=S_state.ap()[(h * 2 + 1) * 128:(h * 2 + 2) * 128,
                                             q_ * 4:q_ * 4 + 4],
                            in_=PUB[:, q_ * 4:q_ * 4 + 4]).then_inc(H["pub"], 16)
            return None
        s.op("sync", pubd, waits=[("dvy", s.c("dvy"))], minc=("pub", 16))

        def pubf(e, f=fidx):
            with e.If_eq(SR["en"], 0):
                e.dma_start(out=S_flag.ap()[0:1, f * 2:f * 2 + 1],
                            in_=CONS[0:1, f:f + 1]).then_inc(H["pub"], 16)
            with e.Else():
                e.dma_start(out=S_flag.ap()[0:1, f * 2 + 1:f * 2 + 2],
                            in_=CONS[0:1, f:f + 1]).then_inc(H["pub"], 16)
            return None
        s.op("sync", pubf, waits=[("pub", s.c("pub"))], minc=("pub", 16))

    def emit_ssm_ns(hh, n):
        pub0 = CH['pub0']
        if n % 4 == 0:
            emit_ssm_icols(hh, n // 4)
        if True:
            idx = hh * 16 + n
            G = idx
            ab = AB[n % 2]
            mt = MPP[n % 2]
            m0, m1 = mt[:, 0:TH], mt[:, TH:T]
            adt = T1B[n % 2]
            ad0, ad1 = adt[:, 0:TH], adt[:, TH:T]
            ax_w = [("dvc", hh + 1)]
            if G >= 2:
                ax_w += [("pot", 2 * (G - 1)), ("dvn", 2 * (G - 1))]
            s.op("scalar", lambda e, h=hh, i=idx, a=ab: e.activation(
                a[:, :], DTT[h % 2][:, :], AF.Exp, scale=At[:, i:i + 1]), waits=ax_w, inc=("ax", 1))

            pw = [("ax", s.c("ax")), ("dus", s.c("dus"))]
            if G >= 2:
                pw.append(("dvn", 2 * G - 3))
            s.op("gpsimd", lambda e, a=ab, d0=ad0: e.tensor_mul(
                d0, a[:, 0:TH], DU[:, 0:TH]), waits=pw, inc=("pot", 1))
            pw2 = []
            if G >= 2:
                pw2.append(("dvn", 2 * G - 2))
            s.op("gpsimd", lambda e, a=ab, d1=ad1: e.tensor_mul(
                d1, a[:, TH:T], DU[:, TH:T]), waits=pw2, inc=("pot", 1))

            sw = [("pot", s.c("pot") - 1), ("dvi", s.c("dvi"))]
            if G >= 2:
                sw += [("dvq", G - 1), ("dvy", G - 1)]
            s.op("vector", lambda e, a=ab, mm0=m0, d0=ad0, n_=n: e.tensor_tensor_scan(
                mm0, a[:, 0:TH], d0, ICOLS[:, n_:n_ + 1], OP.mult, OP.add),
                waits=sw, inc=("dvn", 1))
            sw2 = [("dvn", s.c("dvn")), ("pot", s.c("pot"))]
            if G >= 2:
                sw2.append(("dvy", G - 1))
            s.op("vector", lambda e, a=ab, m=mt, d1=ad1: e.tensor_tensor_scan(
                m[:, TH:T], a[:, TH:T], d1, m[:, TH - 1:TH],
                OP.mult, OP.add), waits=sw2, inc=("dvn", 1))

            yw = [("dvs", s.c("dvs"))] if n == 0 else []
            s.op("vector", lambda e, i=idx, m=mt: e.scalar_tensor_tensor(
                YAC[:, :], m[:, :], Cc[:, i:i + 1], YAC[:, :], OP.mult, OP.add),
                waits=yw, inc=("dvq", 1))
            s.op("vector", lambda e, h=hh, n_=n, m=mt: e.tensor_sub(
                PUB[:, n_:n_ + 1], m[:, T - 1:T], XNT[h][:, T - 1:T]),
                waits=[("dvn", 2 * G + 2), ("pub", pub0)], inc=("dvy", 1))
        if n % 4 == 3:
            emit_ssm_pubq(hh, n // 4)

    def emit_ssm_tail(hh):
        if not do_y:
            return
        YBF = XTK
        s.op("scalar", lambda e: e.copy(YBF[:, :], YAC[:, :]),
             waits=[("dvy", s.c("dvy")), ("ytr", s.c("ytr")), ("pe_xp", NT)], inc=("dvyc", 1))
        ya0 = s.c("ya")
        for g in range(4):
            sv = g % 2
            for k in range(4):
                tt = 4 * g + k
                src_b = YBF[:, tt * 128:(tt + 1) * 128]
                dst_b = YSCR4[sv][:, k * 128:(k + 1) * 128]
                tw = ["dvyc", ("ytr", max(0, s.c("ytr") - 32))]
                if k == 0 and g >= 2:
                    tw.append(("ya", ya0 + 16 * (g - 1)))
                s.op("sync", lambda e, d=dst_b, sr=src_b: e.dma_start_transpose(d, sr),
                     waits=tw, inc=("ytr", 16))
            yw = [("ytr", s.c("ytr")), ("sob", SOB_ALL)]
            if hh >= 4 and g < 2:
                yw.append(("soa", 128))
            s.op("gpsimd", lambda e, g_=g, h=hh, sv_=sv: e.dma_start(
                out=d_out[g_ * 512:(g_ + 1) * 512,
                          h * 128:(h + 1) * 128].rearrange("(b p) h -> p b h", p=128),
                in_=YSCR4[sv_].ap().rearrange("p (b h) -> p b h", b=4),
                accum_op=OP.add), waits=yw, inc=("ya", 16))

    SCT2 = [UROL[i // 2].ap().bitcast(BF16)[:, (i % 2) * TH:(i % 2) * TH + TH]
            for i in range(4)]
    AE_DONE = {}
    LAST_SET_USE = {}

    def SCT_OF(sl):
        return SCT2 if (sl % 2 == 1 and sl >= 5) else SCT

    def emit_expert_mm1(sl):
        half, e_idx = sl // NE, sl % NE
        toff = half * TH
        par = sl % 2
        sct = SCT_OF(sl)
        for fc in range(4):
            st = fc % 2

            def mm1(e, f=fc, p=par, to=toff, stt=st):
                last = None
                for k in range(HH):
                    lhs = KEYSB[p][:, k * 512 + f * 128:k * 512 + f * 128 + 128]
                    for slab in range(2):
                        last = e.matmul(PS_SC[stt].ap()[:, slab * 512:slab * 512 + 512],
                                        lhs, XNT[k][:, to + slab * 512:to + slab * 512 + 512],
                                        start=(k == 0), stop=(k == HH - 1))
                return last
            waits = [("wk", 16 * (sl + 1)), ("dvv", 8 if sl < 32 else 16)]
            if sc_free[st]:
                waits.append(sc_free[st])
            if fc < 2 and sl >= 1:
                waits.append(("dvga", 8 * (sl - 1) + fc + 1))
            s.op("tensor", mm1, waits=waits, inc=("pe_mm1", 1))
            last_use = LAST_SET_USE.get(id(sct), -1)
            ew = ["pe_mm1", ("pe_mm2", 8 * (last_use + 1) if last_use >= 0 else 0)]
            if sct is SCT2 and last_use < 0:
                ew.append(("sob", SOB_ALL))
            s.op("scalar", lambda e, f=fc, stt=st, c=sct: e.activation(c[f][:, :],
                                                                       PS_SC[stt].ap()[:, :],
                                                                       AF.Relu),
                 waits=ew, inc=("ae", 1))
            sc_free[st] = ("ae", s.c("ae"))
        AE_DONE[sl] = s.c("ae")
        LAST_SET_USE[id(sct)] = sl

    def emit_expert_mm2(sl, ttls):
        half, e_idx = sl // NE, sl % NE
        par = sl % 2
        sct = SCT_OF(sl)
        for ttl in ttls:
            tt = half * 8 + ttl
            oset = ttl % 2
            pso = PS_SC[ttl] if ttl < 2 else PS_O[oset]

            def mm2(e, t=ttl, ps=pso, p=par, c=sct):
                last = None
                for fc in range(4):
                    lhs = c[fc][:, t * 128:(t + 1) * 128]
                    for vh in range(2):
                        last = e.matmul(ps.ap()[:, vh * 512:vh * 512 + 512], lhs,
                                        VALSB[p][:, fc * 1024 + vh * 512:fc * 1024 + vh * 512 + 512],
                                        start=(fc == 0), stop=(fc == 3))
                return last
            if ttl < 2:
                waits = [("ae", s.c("ae")), ("wv", 16 * (sl + 1))]
            else:
                waits = [("ae", AE_DONE[sl]), ("wv", 16 * (sl + 1))]
                if o_free[oset]:
                    waits.append(o_free[oset])
            s.op("tensor", mm2, waits=waits, inc=("pe_mm2", 1))
            gcol = GTS[:, tt * 32 + e_idx:tt * 32 + e_idx + 1]
            if half == 1 and e_idx == 0:
                s.op("vector", lambda e, t=ttl, ps=pso, g=gcol: e.tensor_single_scalar(
                    ACC[t][:, :], ps.ap()[:, :], g, OP.mult),
                    waits=["pe_mm2", ("dvt", GATE_DVT[half * 8 + ttl]),
                           ("soa", 16 * (ttl + 1))], inc=("dvga", 1))
            else:
                s.op("vector", lambda e, t=ttl, ps=pso, g=gcol: e.scalar_tensor_tensor(
                    ACC[t][:, :], ps.ap()[:, :], g, ACC[t][:, :], OP.mult, OP.add),
                    waits=["pe_mm2", ("dvt", GATE_DVT[half * 8 + ttl]), ("dva", 8)],
                    inc=("dvga", 1))
            if ttl >= 2:
                o_free[oset] = ("dvga", s.c("dvga"))

    def emit_accdma(half):
        ya_now = s.c("ya")
        for ttl in range(8):
            tt = half * 8 + ttl
            if dbg and half == 0:
                s.op("gpsimd", lambda e, j=ttl: e.dma_start(
                    out=d_dbg[j * 128:(j + 1) * 128, :], in_=ACC[j][:, :]),
                    waits=[("dvga", s.c("dvga"))], inc=("sdb", 16))
            s.op("gpsimd", lambda e, t=tt, j=ttl: e.dma_start(
                out=d_out[t * 128:(t + 1) * 128, :], in_=ACC[j][:, :], accum_op=OP.add),
                waits=[("dvga", s.c("dvga")), ("ya", ya_now), ("sob", SOB_ALL)],
                inc=("soa", 16))
        if dbg and half == 0:
            s.op("gpsimd", lambda e: e.dma_start(out=d_dbg2, in_=GTS[:, :]),
                 waits=[("dvt", s.c("dvt"))], inc=("sdb", 16))

    OLD_HEAD = {8 * hh + 1: hh for hh in range(HH)}
    HEAD_AT = {sl - 1: hh for sl, hh in OLD_HEAD.items()}
    WDT_AT = {8: 2, 16: 3, 24: 4, 32: 5, 40: 6, 48: 7}
    NS_AT = {}
    TAIL_AT = {}
    for sl0, hh0 in OLD_HEAD.items():
        batches = [[0, 1], [2, 3, 4], [5, 6, 7], [8, 9, 10], [11, 12, 13], [14, 15]]
        for k, b in enumerate(batches):
            NS_AT.setdefault(sl0 + k, []).append((hh0, b))
        TAIL_AT[sl0 + 5] = hh0
    for tt in range(8):
        emit_ln(tt)
    if do_ffn:
        for tt in range(8):
            emit_gating(tt)
        sc_free[0] = ("ag", s.c("ag"))
        sc_free[1] = None
        emit_expert_mm1(0)
    for tt in range(8, NT):
        emit_ln(tt)
    for tt in range(NT):
        s.op("scalar", lambda e, t=tt: e.dma_start(out=d_out[t * 128:(t + 1) * 128, :],
                                                   in_=d_u[t * 128:(t + 1) * 128, :]),
             inc=("sob", 16))
    XNT_READY = ("dvv", s.c("dvv"))
    SOB_ALL = s.c("sob")
    o_free[0] = ("dvv", s.c("dvv"))
    o_free[1] = ("am", s.c("am"))
    if do_ffn:
        for tt in range(8, NT):
            emit_gating(tt)
        sc_free[0] = ("ag", s.c("ag"))
    for i in range(8):
        s.op("vector", lambda e, j=i: e.memset(ACC[j][:, :], 0.0), inc=("dva", 1))
    if do_ssm:
        emit_wdt(0)
        emit_wdt(1)
    for sl in range(64):
        if do_ffn:
            if sl + 2 < 64:
                load_keys(sl + 2)
            if sl + 1 < 64:
                load_vals(sl + 1)
        if do_ssm and sl in WDT_AT:
            emit_wdt(WDT_AT[sl])
        pieces = []
        if do_ssm:
            for hh0, b in NS_AT.get(sl, []):
                for n0 in b:
                    pieces.append((hh0, n0))
        interleave = sl + 1 >= 6
        if do_ffn:
            emit_expert_mm2(sl, [0, 1])
        for hh0, n0 in pieces[:2]:
            emit_ssm_ns(hh0, n0)
        if do_ffn:
            if sl + 1 < 64 and interleave:
                emit_expert_mm1(sl + 1)
            emit_expert_mm2(sl, [2, 3, 4, 5])
        for hh0, n0 in pieces[2:]:
            emit_ssm_ns(hh0, n0)
        if do_ffn:
            emit_expert_mm2(sl, [6, 7])
            if sl + 1 < 64 and not interleave:
                emit_expert_mm1(sl + 1)
        if do_ssm and sl in TAIL_AT:
            emit_ssm_tail(TAIL_AT[sl])
        if do_ssm and sl in HEAD_AT:
            emit_ssm_head(HEAD_AT[sl])
        if do_ffn and sl == 31:
            emit_accdma(0)
    if do_ffn:
        emit_accdma(1)

    s.op("sync", lambda e: e.nop(),
         waits=[("sob", s.c("sob")), ("soa", s.c("soa")), ("ya", s.c("ya")),
                ("sdb", s.c("sdb")),
                ("pub", s.c("pub")), ("hs", s.c("hs")),
                ("wk", s.c("wk")), ("wv", s.c("wv"))])

    H = {}
    for name in list(s.counts) + ["pl"]:
        H[name] = nc.alloc_semaphore(name)
    SR = {}
    last = {e: {} for e in ENGINES}

    if return_sched:
        return s
    with nc.Block() as block:
        def sync_body(eng):
            SR["en"] = eng.alloc_register("r_en")
            SR["cnt"] = eng.alloc_register("r_cnt")
            SR["flag"] = eng.alloc_register("r_flag")
            SR["tgt"] = eng.alloc_register("r_tgt")
            SR["diff"] = eng.alloc_register("r_diff")
            s.emit_fn(H, "sync", last)(eng)
        block.sync(sync_body)
        block.vector(s.emit_fn(H, "vector", last))
        block.scalar(s.emit_fn(H, "scalar", last))
        block.tensor(s.emit_fn(H, "tensor", last))
        block.gpsimd(s.emit_fn(H, "gpsimd", last))

    return nc


_NC_CACHE = {}


def kernel(L0_attn_u, L0_attn_A_log, L0_attn_B_p, L0_attn_C_p, L0_attn_D_p,
           L0_attn_W_dt, L0_attn_b_dt, L0_ffn_expert_sel, L0_ffn_keys, L0_ffn_values):
    u = np.ascontiguousarray(np.asarray(L0_attn_u, dtype=np.float32))
    A_log = np.asarray(L0_attn_A_log, np.float32)
    B_p = np.asarray(L0_attn_B_p, np.float32)
    C_p = np.asarray(L0_attn_C_p, np.float32)
    D_p = np.asarray(L0_attn_D_p, np.float32)
    W_dt = np.ascontiguousarray(np.asarray(L0_attn_W_dt, np.float32))
    b_dt = np.asarray(L0_attn_b_dt, np.float32)
    sel = np.asarray(L0_ffn_expert_sel, np.float32)
    keys = np.ascontiguousarray(np.asarray(L0_ffn_keys, np.float32))
    vals = np.ascontiguousarray(np.asarray(L0_ffn_values, np.float32))

    def chunkp(x):
        n = x.shape[1]
        return np.ascontiguousarray(x.reshape(HH, 128, n).transpose(1, 0, 2).reshape(128, HH * n))

    abc = np.concatenate([chunkp(A_log), chunkp(B_p), chunkp(C_p)], axis=1)
    dpb = np.concatenate([np.ascontiguousarray(D_p.reshape(HH, 128).T),
                          np.ascontiguousarray(b_dt.reshape(HH, 128).T)], axis=1)
    selr = np.ascontiguousarray(sel.T.reshape(HH, 128, 32).transpose(1, 0, 2).reshape(128, 256))
    ident = np.eye(128, dtype=np.float32).astype(bfdt)

    nonce = (int.from_bytes(os.urandom(4), "little") & 0x3FFFFFF0) + 16
    if "nc" not in _NC_CACHE:
        _NC_CACHE["nc"] = build_program()
    nc = _NC_CACHE["nc"]

    in_maps = []
    for c in range(8):
        b, half = c // 2, c % 2
        cons = np.zeros((1, 40), np.int32)
        for f in range(4 * HH):
            cons[0, f] = nonce + f
        cons[0, 32] = 1 if half == 1 else 0
        mask = np.full((128, 1), 1.0 if half == 1 else 0.0, np.float32)
        in_maps.append({
            "u": np.ascontiguousarray(u[b, half * T:(half + 1) * T, :]),
            "wdt": W_dt, "abc": abc, "dpb": dpb, "sel": selr,
            "keys": keys, "vals": vals, "ident": ident,
            "cons": cons, "maskc": mask,
        })
    res = run_bass_kernel_spmd(nc, in_maps, core_ids=list(range(8)))
    out = np.empty((4, 4096, 1024), np.float32)
    for c in range(8):
        b, half = c // 2, c % 2
        out[b, half * T:(half + 1) * T, :] = res.results[c]["out"]
    return out
